# version 05
# baseline (speedup 1.0000x reference)
import os
import numpy as np
import concourse.bass as bass
import concourse.mybir as mybir
from concourse.bass_utils import run_bass_kernel_spmd

F32 = mybir.dt.float32
BF16 = mybir.dt.bfloat16
AF = mybir.ActivationFunctionType
ALU = mybir.AluOpType
AX = mybir.AxisListType

NCORES = 8
KPHASE = int(os.environ.get("KPHASE", "5"))
KCC = int(os.environ.get("KCC", "1"))
KKRON = int(os.environ.get("KKRON", "1"))
KKLVL = int(os.environ.get("KKLVL", "4"))
T = 16
NT = 128
D = 1024
NCH = 8
TOK = T * NT


class _Mock:
    def __getattr__(self, k):
        def f(*a, **kw):
            return self
        return f

    def __getitem__(self, k):
        return self


class Ctx:

    def __init__(self, emit, EV, nc=None, sems=None, tensors=None):
        self.emit = emit
        self.EV = EV
        self.nc = nc
        self.sems = sems or {}
        self.t = tensors or {}
        self.cnt = {k: 0 for k in ("sX0", "sX1", "sO0", "sO1", "sW", "sCC", "sT", "sV", "sS", "sP", "sM")}

    def inc(self, res, sem, name, by=1):
        self.cnt[sem] += by
        if self.emit:
            assert self.EV[name] == (sem, self.cnt[sem]), (name, self.EV.get(name), sem, self.cnt[sem])
            assert res is not None, name
            res.then_inc(self.sems[sem], by)
        else:
            assert name not in self.EV, name
            self.EV[name] = (sem, self.cnt[sem])

    def wait(self, eng, name):
        if name is None:
            return
        if self.emit:
            sem, val = self.EV[name]
            eng.wait_ge(self.sems[sem], val)

    def waitv(self, eng, sem, val):
        if val <= 0:
            return
        if self.emit:
            eng.wait_ge(self.sems[sem], val)


def build_program(cx: Ctx):
    emit = cx.emit
    t_ = cx.t
    if emit:
        nc = cx.nc
        gE, syE, sE, vE, tE = cx.block_engines
    else:
        gE = syE = sE = vE = tE = _Mock()

    def tt(name):
        return t_[name] if emit else _Mock()

    x_d = tt("x_d"); y_d = tt("y_d")
    wqt_d = tt("wqt_d"); wkt_d = tt("wkt_d"); wv_d = tt("wv_d"); wo_d = tt("wo_d")
    wf_d = tt("wf_d"); bfrow_d = tt("bfrow_d"); borow_d = tt("borow_d"); brirow_d = tt("brirow_d")
    aR_d = tt("aR_d"); bR_d = tt("bR_d"); aI_d = tt("aI_d"); bI_d = tt("bI_d")
    ident_d = tt("ident_d"); maskbd_d = tt("maskbd_d"); mlow_d = tt("mlow_d"); mhigh_d = tt("mhigh_d")
    ccin = tt("ccin"); ccout = tt("ccout"); wrdram = tt("wrdram"); widram = tt("widram")

    wkf = tt("wkf"); wqf = tt("wqf"); wvb = tt("wvb"); wob = tt("wob"); wffn = tt("wffn")
    wstage = tt("wstage"); ring = tt("ring"); xbuf = tt("xbuf")
    lnbuf = tt("lnbuf"); kfbuf = tt("kfbuf"); vtbuf = tt("vtbuf"); tmpbuf = tt("tmpbuf")
    x2 = tt("x2"); outb = tt("outb")
    bo_bc = tt("bo_bc"); bfr_bc = tt("bfr_bc"); bri_bc = tt("bri_bc")
    kvcc = tt("kvcc"); kf_acc = tt("kf_acc"); kvbd = tt("kvbd"); ksumbd = tt("ksumbd")
    normc = tt("normc"); recip = tt("recip"); junk = tt("junk"); vjunk = tt("vjunk")
    kronst = tt("kronst"); ffntmp = tt("ffntmp")
    identb = tt("identb"); bd2wf = tt("bd2wf"); maskbd_s = tt("maskbd_s")
    mlow = tt("mlow"); mhigh = tt("mhigh")
    aRb = tt("aRb"); bRb = tt("bRb"); aIb = tt("aIb"); bIb = tt("bIb")
    borow_s = tt("borow_s"); bfrow_s = tt("bfrow_s"); brirow_s = tt("brirow_s")
    ones1 = tt("ones1"); ones128 = tt("ones128"); zerov = tt("zerov"); epsv = tt("epsv")
    stA = tt("stA"); stB = tt("stB")
    psA = tt("psA"); psB = tt("psB"); psC = tt("psC"); psD = tt("psD")

    def xwait(eng, ld):
        cx.waitv(eng, f"sX{ld % 2}", 16 * (ld // 2 + 1))

    def gbar(g, name):
        cx.waitv(g, "sW", 16 * (cx.cnt["sW"] // 16))
        cx.inc(g.nop(), "sM", name)

    def st(par, col):
        s = stA if (par % 2) == 0 else stB
        return s[:, col:col + 1] if emit else None

    def prog_g(g):
        g.memset(bd2wf[0:64, 64:128], 0.0)
        g.memset(bd2wf[64:128, 0:64], 0.0)
        cx.inc(g.dma_start(out=identb[:], in_=ident_d[:, :]), "sW", "w_ident", 16)
        cx.inc(g.dma_start(out=bd2wf[0:64, 0:64], in_=wf_d[:, :]), "sW", "w_wf0", 16)
        cx.inc(g.dma_start(out=bd2wf[64:128, 64:128], in_=wf_d[:, :]), "sW", "w_wf1", 16)
        cx.inc(g.dma_start(out=aRb[:], in_=aR_d[:, :]), "sW", "w_aR", 16)
        cx.inc(g.dma_start(out=bRb[:], in_=bR_d[:, :]), "sW", "w_bR", 16)
        cx.inc(g.dma_start(out=aIb[:], in_=aI_d[:, :]), "sW", "w_aI", 16)
        cx.inc(g.dma_start(out=bIb[:], in_=bI_d[:, :]), "sW", "w_bI", 16)
        cx.inc(g.dma_start(out=borow_s[:], in_=borow_d[:, :]), "sW", "w_bo", 16)
        cx.inc(g.dma_start(out=bfrow_s[:], in_=bfrow_d[:, :]), "sW", "w_bf", 16)
        cx.inc(g.dma_start(out=brirow_s[:], in_=brirow_d[:, :]), "sW", "w_bri", 16)
        cx.inc(g.dma_start(out=maskbd_s[:], in_=maskbd_d[:, :]), "sW", "w_mbd", 16)
        cx.inc(g.dma_start(out=mlow[:], in_=mlow_d[:, :]), "sW", "w_ml", 16)
        cx.inc(g.dma_start(out=mhigh[:], in_=mhigh_d[:, :]), "sW", "w_mh", 16)
        gbar(g, "m_smalls")
        for p in range(4):
            if p >= 2:
                cx.wait(g, f"Tck{2 * (p - 2) + 1}b")
            src = wkt_d[p * 256:(p + 1) * 256, :].rearrange("(c q) d -> q c d", c=2) if emit else None
            cx.inc(g.dma_start(out=wstage[:, p % 2], in_=src), "sW", f"w_kt{p}", 16)
            gbar(g, f"m_kt{p}")
        srcv = wv_d[:, :].rearrange("(k q) d -> q k d", k=8) if emit else None
        cx.inc(g.dma_start(out=wvb[:], in_=srcv), "sW", "w_v", 16)
        gbar(g, "m_v")
        srco = wo_d[:, :].rearrange("(k q) d -> q k d", k=8) if emit else None
        cx.inc(g.dma_start(out=wob[:], in_=srco), "sW", "w_o", 16)
        gbar(g, "m_o")
        if KPHASE < 3:
            return
        for p in (0, 1):
            cx.wait(g, f"Tck{2 * (p + 2) + 1}b")
            src = wqt_d[p * 256:(p + 1) * 256, :].rearrange("(c q) d -> q c d", c=2) if emit else None
            cx.inc(g.dma_start(out=wstage[:, p % 2], in_=src), "sW", f"w_qt{p}", 16)
            gbar(g, f"m_qt{p}")
        if KCC:
            cx.wait(g, "Vkv2")
            cx.inc(g.dma_start(out=ccin[:, :], in_=kvcc[:]), "sW", "w_cci", 16)
            cx.waitv(g, "sW", 16 * (cx.cnt["sW"] // 16))
            res = g.collective_compute(
                "AllReduce", ALU.add,
                replica_groups=[[0, 1, 2, 3], [4, 5, 6, 7]],
                ins=[ccin.ap().opt()] if emit else None,
                outs=[ccout.ap().opt()] if emit else None,
            ) if emit else None
            cx.inc(res, "sCC", "CC", 1)
        for p in (2, 3):
            cx.wait(g, f"Tcq{2 * (p - 2) + 1}b")
            src = wqt_d[p * 256:(p + 1) * 256, :].rearrange("(c q) d -> q c d", c=2) if emit else None
            cx.inc(g.dma_start(out=wstage[:, p % 2], in_=src), "sW", f"w_qt{p}", 16)
            gbar(g, f"m_qt{p}")
        if KKRON and KKLVL >= 3:
          for w, dram in (("R", wrdram), ("I", widram)):
            for q in range(8):
                cx.wait(g, f"Vkron_{w}{q}")
                cx.inc(g.dma_start(out=dram[q * 128:(q + 1) * 128, :], in_=kronst[:, q]),
                       "sW", f"w_kd_{w}{q}", 16)
            gbar(g, f"m_kd{w}")
        if KKRON and KKLVL >= 4:
          for w, dram in (("R", wrdram), ("I", widram)):
            src = dram[:, :].rearrange("(b d) (g e) -> (d g) b e", b=32, d=32, g=16, e=16) if emit else None
            for k in range(4):
                if w == "R":
                    cx.inc(g.dma_start(out=wffn[:, k, 0:512], in_=src[k * 128:(k + 1) * 128] if emit else None),
                           "sW", f"w_fixRa{k}", 16)
                    cx.inc(g.dma_start(out=wffn[:, 4 + k, 512:1024], in_=src[k * 128:(k + 1) * 128] if emit else None),
                           "sW", f"w_fixRb{k}", 16)
                else:
                    cx.inc(g.dma_start(out=wffn[:, k, 512:1024], in_=src[k * 128:(k + 1) * 128] if emit else None),
                           "sW", f"w_fixIa{k}", 16)
                    cx.inc(g.dma_start(out=ffntmp[:, k], in_=src[k * 128:(k + 1) * 128] if emit else None),
                           "sW", f"w_fixIb{k}", 16)
        if KKRON and KKLVL >= 4:
            gbar(g, "m_fix")
        if KCC:
            cx.wait(g, "CC")
            cx.inc(g.dma_start(out=kvcc[:], in_=ccout[:, :]), "sW", "w_cco", 16)
            gbar(g, "m_cco")
        if KPHASE < 5:
            return
        for t in range(T):
            if t == 0:
                cx.wait(g, "Vb_bo")
                cx.wait(g, "Vb_bri")
            xwait(g, T + t)
            cx.inc(g.tensor_tensor(out=xbuf[:, t % 2], in0=xbuf[:, t % 2], in1=bo_bc[:], op=ALU.add),
                   "sP", f"Pbo{t}")
            if t >= 1:
                cx.wait(g, f"Vln2_{t - 1}")
                cx.inc(g.tensor_tensor(out=x2[:, (t - 1) % 2], in0=x2[:, (t - 1) % 2], in1=bri_bc[:], op=ALU.add),
                       "sP", f"Pbri{t - 1}")
        cx.wait(g, f"Vln2_{T - 1}")
        cx.inc(g.tensor_tensor(out=x2[:, (T - 1) % 2], in0=x2[:, (T - 1) % 2], in1=bri_bc[:], op=ALU.add),
               "sP", f"Pbri{T - 1}")

    def prog_sy(sy):
        if KPHASE < 2:
            return
        for t in range(T):
            if t >= 2:
                cx.wait(sy, f"Vln{t - 2}")
                cx.wait(sy, f"Sln{t - 2}")
            cx.inc(sy.dma_start(out=xbuf[:, t % 2], in_=x_d[t * NT:(t + 1) * NT, :]),
                   f"sX{t % 2}", f"X{t}", 16)
        if KPHASE < 5:
            return
        for t in range(T):
            if t >= 2:
                cx.wait(sy, f"Vwo{t - 2}")
            else:
                cx.wait(sy, f"Vln{14 + t}")
                cx.wait(sy, f"Sln{14 + t}")
            if t >= 2:
                cx.wait(sy, f"Vout{t - 2}")
                cx.inc(sy.dma_start(out=y_d[(t - 2) * NT:(t - 1) * NT, :], in_=outb[:, t % 2]),
                       f"sO{t % 2}", f"O{t - 2}", 16)
            cx.inc(sy.dma_start(out=xbuf[:, t % 2], in_=x_d[t * NT:(t + 1) * NT, :]),
                   f"sX{t % 2}", f"XB{t}", 16)
        for t in (T - 2, T - 1):
            cx.wait(sy, f"Vout{t}")
            cx.inc(sy.dma_start(out=y_d[t * NT:(t + 1) * NT, :], in_=outb[:, t % 2]),
                   f"sO{t % 2}", f"O{t}", 16)
        cx.waitv(sy, "sO0", 16 * (T // 2))
        cx.waitv(sy, "sO1", 16 * (T // 2))

    def ln_chain(s, par, phase, t):
        s.activation(out=st(par, 2), in_=st(par, 0), func=AF.Identity, scale=1.0 / D, bias=zerov[:] if emit else None)
        s.activation(out=st(par, 3), in_=st(par, 2), func=AF.Square)
        s.activation(out=st(par, 4), in_=st(par, 3), func=AF.Identity, scale=-1.0, bias=epsv[:] if emit else None)
        s.activation(out=st(par, 5), in_=st(par, 1), func=AF.Identity, scale=1.0 / D, bias=st(par, 4))
        s.activation(out=st(par, 6), in_=st(par, 5), func=AF.Ln)
        res = s.activation(out=st(par, 7), in_=st(par, 6), func=AF.Exp, scale=-0.5, bias=zerov[:] if emit else None)
        cx.inc(res, "sS", f"S{phase}{t}")

    def prog_s(s):
        cx.wait(s, "Vconst")
        if KPHASE < 2:
            return
        for t in range(T):
            xwait(s, t)
            s.activation(out=junk[:], in_=xbuf[:, t % 2], func=AF.Square, accum_out=st(t, 1))
            s.activation(out=junk[:], in_=xbuf[:, t % 2], func=AF.Identity, scale=1.0,
                         bias=zerov[:] if emit else None, accum_out=st(t, 0))
            ln_chain(s, t, "ln", t)
            cx.wait(s, f"Vkf{t}")
            if t >= 2:
                cx.wait(s, f"Tkv{t - 2}")
            res = s.activation(out=kfbuf[:, t % 2], in_=tmpbuf[:, t % 2], func=AF.Relu)
            cx.inc(res, "sS", f"Skf{t}")
        if KPHASE < 3:
            return
        for t in range(T):
            cx.wait(s, f"Vqf{t}")
            cx.wait(s, f"Tqtr{t - 2}" if t >= 2 else f"Tkv{T - 1}")
            res = s.activation(out=kfbuf[:, t % 2], in_=tmpbuf[:, t % 2], func=AF.Relu)
            cx.inc(res, "sS", f"Sqf{t}")
        if KPHASE < 5:
            return
        for t in range(T):
            cx.wait(s, f"Vwo{t}")
            s.activation(out=junk[:], in_=x2[:, t % 2], func=AF.Square, accum_out=st(t, 1))
            s.activation(out=junk[:], in_=x2[:, t % 2], func=AF.Identity, scale=1.0,
                         bias=zerov[:] if emit else None, accum_out=st(t, 0))
            ln_chain(s, t, "ln2", t)

    def prog_v(v):
        v.memset(zerov[:], 0.0)
        v.memset(epsv[:], 1e-5)
        v.memset(ones1[:], 1.0)
        v.memset(ones128[:], 1.0)
        v.memset(kf_acc[:], 0.0)
        cx.inc(v.memset(ksumbd[:], 0.0), "sV", "Vconst")
        for nm, dst in (("bo", bo_bc), ("bf", bfr_bc), ("bri", bri_bc)):
            cx.wait(v, f"Tb_{nm}")
            cx.inc(v.tensor_copy(dst[:], psA[:, 0:1024]), "sV", f"Vb_{nm}")
        for c in range(NCH):
            ps = psA if c % 2 == 0 else psB
            for h in range(2):
                cx.wait(v, f"Tck{c}{'ab'[h]}")
                dst = wkf[:, 4 * h:4 * h + 4, c * 128:(c + 1) * 128] if emit else None
                srcv = ps[:, h * 512:(h + 1) * 512].rearrange("p (r q) -> p r q", q=128) if emit else None
                cx.inc(v.tensor_copy(dst, srcv), "sV", f"Vconsk{c}h{h}")
        if KPHASE < 2:
            return
        for t in range(T):
            par = t % 2
            xwait(v, t)
            cx.wait(v, f"Sln{t}")
            if t >= 2:
                cx.wait(v, f"Ttr{t - 2}")
            cx.inc(v.tensor_scalar(out=lnbuf[:, par], in0=xbuf[:, par], scalar1=st(t, 2),
                                   scalar2=st(t, 7), op0=ALU.subtract, op1=ALU.mult),
                   "sV", f"Vln{t}")
            cx.wait(v, f"Ttr{t}")
            cx.inc(v.tensor_copy(ring[:, t], psA[:, 0:1024]), "sV", f"VlnT{t}")
            cx.wait(v, f"Tkf{t}")
            if t >= 2:
                cx.wait(v, f"Skf{t - 2}")
            cx.inc(v.tensor_tensor(out=tmpbuf[:, par], in0=psB[:, 0:1024], in1=bfr_bc[:], op=ALU.add),
                   "sV", f"Vkf{t}")
            cx.wait(v, f"Tv{t}")
            if t >= 2:
                cx.wait(v, f"Tkv{t - 2}")
            cx.inc(v.tensor_copy(vtbuf[:, par], psC[:, 0:1024]), "sV", f"Vv{t}")
            cx.wait(v, f"Skf{t}")
            cx.inc(v.tensor_tensor(out=kf_acc[:], in0=kf_acc[:], in1=kfbuf[:, par], op=ALU.add),
                   "sV", f"Vacc{t}")
        cx.wait(v, f"Tkv{T - 1}")
        cx.inc(v.tensor_copy(kvcc[:, 0:1024], psD[:, 0:1024]), "sV", "Vkv1")
        cx.wait(v, "Tks")
        cx.inc(v.tensor_copy(kvcc[:, 1024:1032], psC[:, 0:8]), "sV", "Vkv2")
        if KPHASE < 3:
            return
        for c in range(NCH):
            ps = psA if c % 2 == 0 else psB
            for h in range(2):
                cx.wait(v, f"Tcq{c}{'ab'[h]}")
                dst = wqf[:, 4 * h:4 * h + 4, c * 128:(c + 1) * 128] if emit else None
                srcv = ps[:, h * 512:(h + 1) * 512].rearrange("p (r q) -> p r q", q=128) if emit else None
                cx.inc(v.tensor_copy(dst, srcv), "sV", f"Vconsq{c}h{h}")
        for t in range(T):
            par = t % 2
            cx.wait(v, f"Tqf{t}")
            cx.wait(v, f"Sqf{t - 2}" if t >= 2 else f"Skf{T - 1}")
            cx.inc(v.tensor_tensor(out=tmpbuf[:, par], in0=psB[:, 0:1024], in1=bfr_bc[:], op=ALU.add),
                   "sV", f"Vqf{t}")
            cx.wait(v, f"Tqtr{t}")
            slot = (t + T) % (T + 1)
            cx.inc(v.tensor_copy(ring[:, slot], psA[:, 0:1024]), "sV", f"VqT{t}")
        if KKRON and KKLVL >= 2:
          for w in ("R", "I"):
            for q in range(8):
                cx.wait(v, f"Tkron_{w}{q}")
                if w == "I" and q == 0 and KKLVL >= 3:
                    cx.wait(v, "m_kdR")
                src = psB[:, 0:256] if q % 2 == 0 else psB[:, 512:768]
                cx.inc(v.tensor_copy(kronst[:, q], src), "sV", f"Vkron_{w}{q}")
        if KPHASE < 4:
            return
        for k in range(4):
            cx.wait(v, "m_fix")
            cx.inc(v.tensor_scalar(out=wffn[:, 4 + k, 0:512], in0=ffntmp[:, k], scalar1=-1.0,
                                   scalar2=0.0, op0=ALU.mult, op1=ALU.add),
                   "sV", f"Vneg{k}")
        cx.wait(v, "m_cco")
        in0 = kvcc[:, 0:1024].rearrange("p (j q) -> p j q", j=8) if emit else None
        in1 = maskbd_s[:].unsqueeze(1).broadcast_to([128, 8, 128]) if emit else None
        out0 = kvbd[:].rearrange("p (j q) -> p j q", j=8) if emit else None
        v.tensor_tensor(out=out0, in0=in0, in1=in1, op=ALU.mult)
        for c in range(NCH):
            v.tensor_scalar(out=ksumbd[:, c, 2 * c:2 * c + 1], in0=kvcc[:, 1024 + c:1025 + c],
                            scalar1=mlow[:] if emit else None, scalar2=None, op0=ALU.mult)
            res = v.tensor_scalar(out=ksumbd[:, c, 2 * c + 1:2 * c + 2], in0=kvcc[:, 1024 + c:1025 + c],
                                  scalar1=mhigh[:] if emit else None, scalar2=None, op0=ALU.mult)
        v.memset(vjunk[:], 0.0)
        cx.inc(v.memset(vjunk[:], 0.0), "sV", "VA3a")
        cx.wait(v, "Tnorm")
        for t in range(T):
            cx.inc(v.tensor_scalar(out=normc[:, t * 16:(t + 1) * 16], in0=psC[:, t * 16:(t + 1) * 16],
                                   scalar1=1e-6, scalar2=None, op0=ALU.max),
                   "sV", f"Vnm{t}")
        v.memset(vjunk[:], 0.0)
        v.memset(vjunk[:], 0.0)
        for t in range(T):
            v.reciprocal(recip[:, t * 16:(t + 1) * 16], normc[:, t * 16:(t + 1) * 16])
        v.memset(vjunk[:], 0.0)
        cx.inc(v.memset(vjunk[:], 0.0), "sV", "Va3end")
        if KPHASE < 5:
            return
        for t in range(T):
            par = t % 2
            cx.wait(v, f"Tein{t}")
            if t >= 2:
                cx.wait(v, f"Tattr{t - 2}")
            in0 = psA[:, 0:1024].rearrange("p (h d) -> p h d", h=16) if emit else None
            in1 = recip[:, t * 16:(t + 1) * 16].unsqueeze(2).broadcast_to([128, 16, 64]) if emit else None
            out0 = kfbuf[:, par].rearrange("p (h d) -> p h d", h=16) if emit else None
            cx.inc(v.tensor_tensor(out=out0, in0=in0, in1=in1, op=ALU.mult), "sV", f"Vein{t}")
            cx.wait(v, f"Tattr{t}")
            if t >= 2:
                cx.wait(v, f"Two{t - 2}")
            cx.inc(v.tensor_copy(vtbuf[:, par], psB[:, 0:1024]), "sV", f"VaT{t}")
            cx.wait(v, f"Two{t}")
            cx.wait(v, f"Pbo{t}")
            if t >= 2:
                cx.wait(v, f"Pbri{t - 2}")
                cx.wait(v, f"Sln2{t - 2}")
            cx.inc(v.scalar_tensor_tensor(out=x2[:, par], in0=psC[:, 0:1024], scalar=1.0,
                                          in1=xbuf[:, par], op0=ALU.mult, op1=ALU.add),
                   "sV", f"Vwo{t}")
            cx.wait(v, f"Sln2{t}")
            if t >= 2:
                cx.wait(v, f"Tx2tr{t - 2}")
            cx.inc(v.tensor_scalar(out=lnbuf[:, par], in0=x2[:, par], scalar1=st(t, 2),
                                   scalar2=st(t, 7), op0=ALU.subtract, op1=ALU.mult),
                   "sV", f"Vln2_{t}")
            cx.wait(v, f"Tx2tr{t}")
            if t >= 2:
                cx.wait(v, f"Tffn{t - 2}")
            cx.inc(v.tensor_copy(tmpbuf[:, par], psB[:, 0:1024]), "sV", f"Vx2T{t}")
            cx.wait(v, f"Tffn{t}")
            cx.wait(v, f"Pbri{t}")
            if t >= 2:
                cx.wait(v, f"O{t - 2}")
            cx.inc(v.scalar_tensor_tensor(out=outb[:, par], in0=psD[:, 0:1024], scalar=1.0,
                                          in1=x2[:, par], op0=ALU.mult, op1=ALU.add),
                   "sV", f"Vout{t}")

    def prog_t(te):
        for nm, row in (("bo", borow_s), ("bf", bfrow_s), ("bri", brirow_s)):
            cx.wait(te, "Vconst")
            cx.wait(te, "m_smalls")
            if nm == "bf":
                cx.wait(te, "Vb_bo")
            if nm == "bri":
                cx.wait(te, "Vb_bf")
            te.matmul(psA[:, 0:512], ones1[:], row[:, 0:512], start=True, stop=True)
            res = te.matmul(psA[:, 512:1024], ones1[:], row[:, 512:1024], start=True, stop=True)
            cx.inc(res, "sT", f"Tb_{nm}")
        cx.wait(te, "m_smalls")
        for c in range(NCH):
            p, cc = c // 2, c % 2
            cx.wait(te, f"m_kt{p}")
            ps = psA if c % 2 == 0 else psB
            if c >= 2:
                cx.wait(te, f"Vconsk{c - 2}h0")
                cx.wait(te, f"Vconsk{c - 2}h1")
            elif c == 0:
                cx.wait(te, "Vb_bri")
            for h in range(2):
                for rq in range(4 * h, 4 * h + 4):
                    lhs = wstage[:, p % 2, cc, rq * 128:(rq + 1) * 128] if emit else None
                    res = te.matmul(ps[:, h * 512 + (rq % 4) * 128: h * 512 + (rq % 4) * 128 + 128],
                                    lhs, bd2wf[:], start=(rq % 4 == 0), stop=(rq % 4 == 3))
                    if rq % 4 == 3:
                        cx.inc(res, "sT", f"Tck{c}{'ab'[h]}")
        if KPHASE < 2:
            return
        for t in range(T):
            par = t % 2
            cx.wait(te, f"Vln{t}")
            cx.wait(te, f"VlnT{t - 1}" if t >= 1 else "Vconsk6h0")
            if t == 0:
                cx.wait(te, "Vconsk6h1")
                cx.wait(te, "m_smalls")
            for c in range(NCH):
                res = te.matmul(psA[:, c * 128:(c + 1) * 128], lnbuf[:, par, c * 128:(c + 1) * 128] if emit else None,
                                identb[:], start=(c % 4 == 0), stop=(c % 4 == 3))
            cx.inc(res, "sT", f"Ttr{t}")
            cx.wait(te, f"VlnT{t}")
            if t >= 1:
                cx.wait(te, f"Vkf{t - 1}")
            else:
                cx.wait(te, "Vconsk7h0")
                cx.wait(te, "Vconsk7h1")
            for h in range(2):
                for k in range(NCH):
                    res = te.matmul(psB[:, h * 512:(h + 1) * 512], ring[:, t, k * 128:(k + 1) * 128] if emit else None,
                                    wkf[:, k, h * 512:(h + 1) * 512] if emit else None,
                                    start=(k == 0), stop=(k == NCH - 1))
            cx.inc(res, "sT", f"Tkf{t}")
            if t == 0:
                cx.wait(te, "m_v")
            if t >= 1:
                cx.wait(te, f"Vv{t - 1}")
            for h in range(2):
                for k in range(NCH):
                    res = te.matmul(psC[:, h * 512:(h + 1) * 512], ring[:, t, k * 128:(k + 1) * 128] if emit else None,
                                    wvb[:, k, h * 512:(h + 1) * 512] if emit else None,
                                    start=(k == 0), stop=(k == NCH - 1))
            cx.inc(res, "sT", f"Tv{t}")
            cx.wait(te, f"Skf{t}")
            cx.wait(te, f"Vv{t}")
            for j in range(8):
                res = te.matmul(psD[:, j * 128:(j + 1) * 128],
                                kfbuf[:, par, j * 128:(j + 1) * 128] if emit else None,
                                vtbuf[:, par, j * 128:(j + 1) * 128] if emit else None,
                                start=(t == 0 and j % 4 == 0), stop=(t == T - 1 and j % 4 == 3))
            cx.inc(res, "sT", f"Tkv{t}")
        cx.wait(te, f"Vacc{T - 1}")
        for c in range(NCH):
            res = te.matmul(psC[:, c:c + 1], kf_acc[:, c * 128:(c + 1) * 128] if emit else None,
                            ones128[:], start=(c == 0), stop=(c == NCH - 1))
        cx.inc(res, "sT", "Tks")
        if KPHASE < 3:
            return
        for c in range(NCH):
            p, cc = c // 2, c % 2
            cx.wait(te, f"m_qt{p}")
            ps = psA if c % 2 == 0 else psB
            if c >= 2:
                cx.wait(te, f"Vconsq{c - 2}h0")
                cx.wait(te, f"Vconsq{c - 2}h1")
            elif c == 0:
                cx.wait(te, f"VlnT{T - 1}")
            else:
                cx.wait(te, f"Vkf{T - 1}")
            for h in range(2):
                for rq in range(4 * h, 4 * h + 4):
                    lhs = wstage[:, p % 2, cc, rq * 128:(rq + 1) * 128] if emit else None
                    res = te.matmul(ps[:, h * 512 + (rq % 4) * 128: h * 512 + (rq % 4) * 128 + 128],
                                    lhs, bd2wf[:], start=(rq % 4 == 0), stop=(rq % 4 == 3))
                    if rq % 4 == 3:
                        cx.inc(res, "sT", f"Tcq{c}{'ab'[h]}")
        for t in range(T):
            cx.wait(te, f"Vqf{t - 1}" if t >= 1 else "Vconsq7h1")
            if t == 0:
                cx.wait(te, "Vconsq7h0")
            for h in range(2):
                for k in range(NCH):
                    res = te.matmul(psB[:, h * 512:(h + 1) * 512], ring[:, t, k * 128:(k + 1) * 128] if emit else None,
                                    wqf[:, k, h * 512:(h + 1) * 512] if emit else None,
                                    start=(k == 0), stop=(k == NCH - 1))
            cx.inc(res, "sT", f"Tqf{t}")
            cx.wait(te, f"Sqf{t}")
            cx.wait(te, f"VqT{t - 1}" if t >= 1 else "Vconsq6h0")
            if t == 0:
                cx.wait(te, "Vconsq6h1")
            for c in range(NCH):
                res = te.matmul(psA[:, c * 128:(c + 1) * 128],
                                kfbuf[:, t % 2, c * 128:(c + 1) * 128] if emit else None,
                                identb[:], start=(c % 4 == 0), stop=(c % 4 == 3))
            cx.inc(res, "sT", f"Tqtr{t}")
        if KKRON:
          for w, (a, b) in (("R", (aRb, bRb)), ("I", (aIb, bIb))):
            cx.wait(te, "m_smalls")
            for q in range(8):
                if q >= 2 and KKLVL >= 2:
                    cx.wait(te, f"Vkron_{w}{q - 2}")
                elif w == "R":
                    cx.wait(te, f"Vqf{T - 1}")
                elif KKLVL >= 2:
                    cx.wait(te, f"Vkron_R{6 + q}")
                dst = psB[:, 0:256] if q % 2 == 0 else psB[:, 512:768]
                res = te.matmul(dst, a[:, q * 128:(q + 1) * 128], b[:], start=True, stop=True)
                cx.inc(res, "sT", f"Tkron_{w}{q}")
        if KPHASE < 4:
            return
        cx.wait(te, "VA3a")
        cx.wait(te, "Vkv2")
        for t in range(T):
            slot = (t + T) % (T + 1)
            for c in range(NCH):
                res = te.matmul(psC[:, t * 16:(t + 1) * 16], ring[:, slot, c * 128:(c + 1) * 128] if emit else None,
                                ksumbd[:, c], start=(c == 0), stop=(c == NCH - 1))
        cx.inc(res, "sT", "Tnorm")
        if KPHASE < 5:
            return
        for t in range(T):
            par = t % 2
            cx.wait(te, "Va3end" if t == 0 else f"Vein{t - 1}")
            if t == 0:
                cx.wait(te, f"VqT{T - 1}")
            for c in range(NCH):
                slot = (t + T) % (T + 1)
                res = te.matmul(psA[:, c * 128:(c + 1) * 128], ring[:, slot, c * 128:(c + 1) * 128] if emit else None,
                                kvbd[:, c * 128:(c + 1) * 128] if emit else None,
                                start=(c % 4 == 0), stop=(c % 4 == 3))
            cx.inc(res, "sT", f"Tein{t}")
            cx.wait(te, f"Vein{t}")
            cx.wait(te, f"VaT{t - 1}" if t >= 1 else "Vkron_I7")
            for c in range(NCH):
                res = te.matmul(psB[:, c * 128:(c + 1) * 128],
                                kfbuf[:, par, c * 128:(c + 1) * 128] if emit else None,
                                identb[:], start=(c % 4 == 0), stop=(c % 4 == 3))
            cx.inc(res, "sT", f"Tattr{t}")
            cx.wait(te, f"VaT{t}")
            cx.wait(te, f"Vwo{t - 1}" if t >= 1 else "Va3end")
            if t == 0:
                cx.wait(te, "m_o")
            for h in range(2):
                for k in range(NCH):
                    res = te.matmul(psC[:, h * 512:(h + 1) * 512], vtbuf[:, par, k * 128:(k + 1) * 128] if emit else None,
                                    wob[:, k, h * 512:(h + 1) * 512] if emit else None,
                                    start=(k == 0), stop=(k == NCH - 1))
            cx.inc(res, "sT", f"Two{t}")
            cx.wait(te, f"Vln2_{t}")
            for c in range(NCH):
                res = te.matmul(psB[:, c * 128:(c + 1) * 128],
                                lnbuf[:, par, c * 128:(c + 1) * 128] if emit else None,
                                identb[:], start=(c % 4 == 0), stop=(c % 4 == 3))
            cx.inc(res, "sT", f"Tx2tr{t}")
            cx.wait(te, f"Vx2T{t}")
            cx.wait(te, f"Vout{t - 1}" if t >= 1 else "Vkv1")
            if t == 0:
                cx.wait(te, "m_fix")
                for k in range(4):
                    cx.wait(te, f"Vneg{k}")
            for h in range(2):
                for k in range(NCH):
                    res = te.matmul(psD[:, h * 512:(h + 1) * 512], tmpbuf[:, par, k * 128:(k + 1) * 128] if emit else None,
                                    wffn[:, k, h * 512:(h + 1) * 512] if emit else None,
                                    start=(k == 0), stop=(k == NCH - 1))
            cx.inc(res, "sT", f"Tffn{t}")

    if emit:
        block = cx.block
        block.gpsimd(prog_g)
        block.sync(prog_sy)
        block.scalar(prog_s)
        block.vector(prog_v)
        block.tensor(prog_t)
    else:
        prog_g(gE)
        prog_sy(syE)
        prog_s(sE)
        prog_v(vE)
        prog_t(tE)


def make_nc(EV):
    nc = bass.Bass("TRN2", target_bir_lowering=False, debug=False)
    nc.detect_race_conditions = False
    t_ = {}
    t_["x_d"] = nc.dram_tensor("x", [TOK, D], F32, kind="ExternalInput").ap()
    t_["y_d"] = nc.dram_tensor("y", [TOK, D], F32, kind="ExternalOutput").ap()
    t_["wqt_d"] = nc.dram_tensor("wqt", [D, D], F32, kind="ExternalInput").ap()
    t_["wkt_d"] = nc.dram_tensor("wkt", [D, D], F32, kind="ExternalInput").ap()
    t_["wv_d"] = nc.dram_tensor("wv", [D, D], F32, kind="ExternalInput").ap()
    t_["wo_d"] = nc.dram_tensor("wo", [D, D], F32, kind="ExternalInput").ap()
    t_["wf_d"] = nc.dram_tensor("wf", [64, 64], F32, kind="ExternalInput").ap()
    t_["bfrow_d"] = nc.dram_tensor("bfrow", [1, D], F32, kind="ExternalInput").ap()
    t_["borow_d"] = nc.dram_tensor("borow", [1, D], F32, kind="ExternalInput").ap()
    t_["brirow_d"] = nc.dram_tensor("brirow", [1, D], F32, kind="ExternalInput").ap()
    t_["aR_d"] = nc.dram_tensor("aR", [4, 1024], F32, kind="ExternalInput").ap()
    t_["bR_d"] = nc.dram_tensor("bR", [4, 256], F32, kind="ExternalInput").ap()
    t_["aI_d"] = nc.dram_tensor("aI", [4, 1024], F32, kind="ExternalInput").ap()
    t_["bI_d"] = nc.dram_tensor("bI", [4, 256], F32, kind="ExternalInput").ap()
    t_["ident_d"] = nc.dram_tensor("ident", [128, 128], F32, kind="ExternalInput").ap()
    t_["maskbd_d"] = nc.dram_tensor("maskbd", [128, 128], F32, kind="ExternalInput").ap()
    t_["mlow_d"] = nc.dram_tensor("mlow_in", [128, 1], F32, kind="ExternalInput").ap()
    t_["mhigh_d"] = nc.dram_tensor("mhigh_in", [128, 1], F32, kind="ExternalInput").ap()
    t_["ccin"] = nc.dram_tensor("ccin", [128, 1032], F32)
    t_["ccout"] = nc.dram_tensor("ccout", [128, 1032], F32)
    t_["wrdram"] = nc.dram_tensor("wrdram", [1024, 256], F32)
    t_["widram"] = nc.dram_tensor("widram", [1024, 256], F32)

    from contextlib import ExitStack
    ctx = ExitStack()
    E = ctx.enter_context

    def sb(name, shape, dt):
        t_[name] = E(nc.sbuf_tensor(name, shape, dt))

    sb("wkf", [128, NCH, D], BF16)
    t_["wqf"] = t_["wkf"]
    sb("wvb", [128, NCH, D], BF16)
    sb("wob", [128, NCH, D], BF16)
    sb("wffn", [128, NCH, D], BF16)
    sb("wstage", [128, 2, 2, D], BF16)
    sb("ring", [128, T + 1, D], BF16)
    sb("xbuf", [128, 2, D], F32)
    sb("lnbuf", [128, 2, D], BF16)
    sb("kfbuf", [128, 2, D], BF16)
    sb("vtbuf", [128, 2, D], BF16)
    sb("tmpbuf", [128, 2, D], BF16)
    sb("x2", [128, 2, D], F32)
    sb("outb", [128, 2, D], F32)
    sb("bo_bc", [128, D], F32)
    sb("bfr_bc", [128, D], F32)
    sb("bri_bc", [128, D], F32)
    sb("kvcc", [128, 1032], F32)
    sb("kf_acc", [128, D], F32)
    sb("kvbd", [128, D], BF16)
    sb("ksumbd", [128, 8, 16], BF16)
    sb("normc", [128, 256], F32)
    sb("recip", [128, 256], F32)
    sb("junk", [128, D], BF16)
    sb("vjunk", [128, 16], F32)
    sb("kronst", [128, 8, 256], F32)
    sb("ffntmp", [128, 4, 512], BF16)
    sb("identb", [128, 128], BF16)
    sb("bd2wf", [128, 128], BF16)
    sb("maskbd_s", [128, 128], F32)
    sb("mlow", [128, 1], F32)
    sb("mhigh", [128, 1], F32)
    sb("aRb", [4, 1024], BF16)
    sb("bRb", [4, 256], BF16)
    sb("aIb", [4, 1024], BF16)
    sb("bIb", [4, 256], BF16)
    sb("borow_s", [1, D], F32)
    sb("bfrow_s", [1, D], F32)
    sb("brirow_s", [1, D], F32)
    sb("ones1", [1, 128], F32)
    sb("ones128", [128, 1], F32)
    sb("zerov", [128, 1], F32)
    sb("epsv", [128, 1], F32)
    sb("stA", [128, 16], F32)
    sb("stB", [128, 16], F32)

    def ps(name):
        t_[name] = E(nc.psum_tensor(name, [128, 512 * 2], F32))

    ps("psA"); ps("psB"); ps("psC"); ps("psD")

    sems = {}
    for s in ("sX0", "sX1", "sO0", "sO1", "sW", "sCC", "sT", "sV", "sS", "sP", "sM"):
        sems[s] = E(nc.semaphore(s))
    block = E(nc.Block())

    cx = Ctx(True, EV, nc=nc, sems=sems, tensors=t_)
    cx.block = block
    cx.block_engines = (None,) * 5
    build_program(cx)
    ctx.close()
    return nc


def kernel(L0_attn_x, L0_attn_Wq, L0_attn_Wk, L0_attn_Wv, L0_attn_Wo, L0_attn_bo,
           L0_attn_Wf, L0_attn_bf, L0_ffn_G1_R, L0_ffn_G2_R, L0_ffn_G1_I, L0_ffn_G2_I,
           L0_ffn_bias_real, L0_ffn_bias_imag):
    f = np.float32
    x2d = np.ascontiguousarray(np.asarray(L0_attn_x, dtype=f).reshape(TOK * NCORES, D))
    wqt = np.ascontiguousarray(np.asarray(L0_attn_Wq, dtype=f).T)
    wkt = np.ascontiguousarray(np.asarray(L0_attn_Wk, dtype=f).T)
    wv = np.ascontiguousarray(np.asarray(L0_attn_Wv, dtype=f))
    wo = np.ascontiguousarray(np.asarray(L0_attn_Wo, dtype=f))
    wf = np.ascontiguousarray(np.asarray(L0_attn_Wf, dtype=f))
    bfrow = np.ascontiguousarray(np.tile(np.asarray(L0_attn_bf, dtype=f), 16).reshape(1, D))
    borow = np.ascontiguousarray(np.asarray(L0_attn_bo, dtype=f).reshape(1, D))
    brirow = np.ascontiguousarray(
        np.concatenate([np.asarray(L0_ffn_bias_real, dtype=f), np.asarray(L0_ffn_bias_imag, dtype=f)]).reshape(1, D))
    aR = np.ascontiguousarray(np.asarray(L0_ffn_G1_R, dtype=f)[0].transpose(1, 0, 2).reshape(4, 1024))
    bR = np.ascontiguousarray(np.asarray(L0_ffn_G2_R, dtype=f)[:, :, 0, :].transpose(0, 2, 1).reshape(4, 256))
    aI = np.ascontiguousarray(np.asarray(L0_ffn_G1_I, dtype=f)[0].transpose(1, 0, 2).reshape(4, 1024))
    bI = np.ascontiguousarray(np.asarray(L0_ffn_G2_I, dtype=f)[:, :, 0, :].transpose(0, 2, 1).reshape(4, 256))
    ident = np.eye(128, dtype=f)
    maskbd = np.kron(np.eye(2, dtype=f), np.ones((64, 64), dtype=f))
    mlow = np.zeros((128, 1), dtype=f); mlow[:64] = 1.0
    mhigh = np.zeros((128, 1), dtype=f); mhigh[64:] = 1.0

    EV = {}
    build_program(Ctx(False, EV))
    nc = make_nc(EV)

    shared = dict(wqt=wqt, wkt=wkt, wv=wv, wo=wo, wf=wf, bfrow=bfrow, borow=borow,
                  brirow=brirow, aR=aR, bR=bR, aI=aI, bI=bI, ident=ident,
                  maskbd=maskbd, mlow_in=mlow, mhigh_in=mhigh)
    in_maps = []
    for i in range(NCORES):
        m = dict(shared)
        m["x"] = np.ascontiguousarray(x2d[i * TOK:(i + 1) * TOK])
        in_maps.append(m)
    res = run_bass_kernel_spmd(nc, in_maps, core_ids=list(range(NCORES)))
    out = np.concatenate([np.asarray(res.results[i]["y"]) for i in range(NCORES)], axis=0)
    return out.reshape(2, 8192, D)


# version 10
# speedup vs baseline: 1.3669x; 1.3669x over previous
import os
import numpy as np
import concourse.bass as bass
import concourse.mybir as mybir
from concourse.bass_utils import run_bass_kernel_spmd

F32 = mybir.dt.float32
BF16 = mybir.dt.bfloat16
AF = mybir.ActivationFunctionType
ALU = mybir.AluOpType
AX = mybir.AxisListType

NCORES = 8
KPHASE = int(os.environ.get("KPHASE", "5"))
KCC = int(os.environ.get("KCC", "1"))
KKRON = int(os.environ.get("KKRON", "1"))
KKLVL = int(os.environ.get("KKLVL", "4"))
KDBG = int(os.environ.get("KDBG", "0"))
KLOCAL = int(os.environ.get("KLOCAL", "0"))
T = 16
NT = 128
D = 1024
NCH = 8
TOK = T * NT


class _Mock:
    def __getattr__(self, k):
        def f(*a, **kw):
            return self
        return f

    def __getitem__(self, k):
        return self


class Ctx:

    def __init__(self, emit, EV, nc=None, sems=None, tensors=None):
        self.emit = emit
        self.EV = EV
        self.nc = nc
        self.sems = sems or {}
        self.t = tensors or {}
        self.cnt = {k: 0 for k in ("sX0", "sX1", "sO0", "sO1", "sW", "sCC", "sT", "sV", "sS", "sP", "sM")}

    def inc(self, res, sem, name, by=1):
        self.cnt[sem] += by
        if self.emit:
            assert self.EV[name] == (sem, self.cnt[sem]), (name, self.EV.get(name), sem, self.cnt[sem])
            assert res is not None, name
            res.then_inc(self.sems[sem], by)
        else:
            assert name not in self.EV, name
            self.EV[name] = (sem, self.cnt[sem])

    def wait(self, eng, name):
        if name is None:
            return
        if self.emit:
            sem, val = self.EV[name]
            eng.wait_ge(self.sems[sem], val)

    def waitv(self, eng, sem, val):
        if val <= 0:
            return
        if self.emit:
            eng.wait_ge(self.sems[sem], val)


def build_program(cx: Ctx):
    emit = cx.emit
    t_ = cx.t
    if emit:
        nc = cx.nc
        gE, syE, sE, vE, tE = cx.block_engines
    else:
        gE = syE = sE = vE = tE = _Mock()

    def tt(name):
        return t_[name] if emit else _Mock()

    x_d = tt("x_d"); y_d = tt("y_d")
    wqt_d = tt("wqt_d"); wkt_d = tt("wkt_d"); wv_d = tt("wv_d"); wo_d = tt("wo_d")
    wf_d = tt("wf_d"); bfrow_d = tt("bfrow_d"); borow_d = tt("borow_d"); brirow_d = tt("brirow_d")
    aR_d = tt("aR_d"); bR_d = tt("bR_d"); aI_d = tt("aI_d"); bI_d = tt("bI_d")
    ident_d = tt("ident_d"); maskbd_d = tt("maskbd_d"); mlow_d = tt("mlow_d"); mhigh_d = tt("mhigh_d")
    ccin = tt("ccin"); ccout = tt("ccout"); wrdram = tt("wrdram"); widram = tt("widram")

    wkf = tt("wkf"); wqf = tt("wqf"); wvb = tt("wvb"); wob = tt("wob"); wffn = tt("wffn")
    wstage = tt("wstage"); ring = tt("ring"); xbuf = tt("xbuf")
    lnbuf = tt("lnbuf"); kfbuf = tt("kfbuf"); vtbuf = tt("vtbuf"); tmpbuf = tt("tmpbuf")
    x2 = tt("x2"); outb = tt("outb")
    bo_bc = tt("bo_bc"); bfr_bc = tt("bfr_bc"); bri_bc = tt("bri_bc")
    kvcc = tt("kvcc"); kf_acc = tt("kf_acc"); kvbd = tt("kvbd"); ksumbd = tt("ksumbd")
    normc = tt("normc"); recip = tt("recip"); junk = tt("junk"); vjunk = tt("vjunk")
    kronst = tt("kronst"); ffntmp = tt("ffntmp")
    identb = tt("identb"); bd2wf = tt("bd2wf"); maskbd_s = tt("maskbd_s")
    mlow = tt("mlow"); mhigh = tt("mhigh")
    aRb = tt("aRb"); bRb = tt("bRb"); aIb = tt("aIb"); bIb = tt("bIb")
    borow_s = tt("borow_s"); bfrow_s = tt("bfrow_s"); brirow_s = tt("brirow_s")
    ones1 = tt("ones1"); ones128 = tt("ones128"); zerov = tt("zerov"); epsv = tt("epsv")
    stA = tt("stA"); stB = tt("stB")
    psA = tt("psA"); psB = tt("psB"); psC = tt("psC"); psD = tt("psD")

    def xwait(eng, ld):
        cx.waitv(eng, f"sX{ld % 2}", 16 * (ld // 2 + 1))

    def gbar(g, name):
        cx.waitv(g, "sW", 16 * (cx.cnt["sW"] // 16))
        cx.inc(g.nop(), "sM", name)

    def st(par, col):
        s = stA if (par % 2) == 0 else stB
        return s[:, col:col + 1] if emit else None

    def prog_g(g):
        g.memset(bd2wf[0:64, 64:128], 0.0)
        g.memset(bd2wf[64:128, 0:64], 0.0)
        cx.inc(g.dma_start(out=identb[:], in_=ident_d[:, :]), "sW", "w_ident", 16)
        cx.inc(g.dma_start(out=bd2wf[0:64, 0:64], in_=wf_d[:, :]), "sW", "w_wf0", 16)
        cx.inc(g.dma_start(out=bd2wf[64:128, 64:128], in_=wf_d[:, :]), "sW", "w_wf1", 16)
        cx.inc(g.dma_start(out=aRb[:], in_=aR_d[:, :]), "sW", "w_aR", 16)
        cx.inc(g.dma_start(out=bRb[:], in_=bR_d[:, :]), "sW", "w_bR", 16)
        cx.inc(g.dma_start(out=aIb[:], in_=aI_d[:, :]), "sW", "w_aI", 16)
        cx.inc(g.dma_start(out=bIb[:], in_=bI_d[:, :]), "sW", "w_bI", 16)
        cx.inc(g.dma_start(out=borow_s[:], in_=borow_d[:, :]), "sW", "w_bo", 16)
        cx.inc(g.dma_start(out=bfrow_s[:], in_=bfrow_d[:, :]), "sW", "w_bf", 16)
        cx.inc(g.dma_start(out=brirow_s[:], in_=brirow_d[:, :]), "sW", "w_bri", 16)
        cx.inc(g.dma_start(out=maskbd_s[:], in_=maskbd_d[:, :]), "sW", "w_mbd", 16)
        cx.inc(g.dma_start(out=mlow[:], in_=mlow_d[:, :]), "sW", "w_ml", 16)
        cx.inc(g.dma_start(out=mhigh[:], in_=mhigh_d[:, :]), "sW", "w_mh", 16)
        gbar(g, "m_smalls")
        for p in range(4):
            if p >= 2:
                cx.wait(g, f"Tck{2 * (p - 2) + 1}b")
            src = wkt_d[p * 256:(p + 1) * 256, :].rearrange("(c q) d -> q c d", c=2) if emit else None
            cx.inc(g.dma_start(out=wstage[:, p % 2], in_=src), "sW", f"w_kt{p}", 16)
            gbar(g, f"m_kt{p}")
        srcv = wv_d[:, :].rearrange("(k q) d -> q k d", k=8) if emit else None
        cx.inc(g.dma_start(out=wvb[:], in_=srcv), "sW", "w_v", 16)
        gbar(g, "m_v")
        srco = wo_d[:, :].rearrange("(k q) d -> q k d", k=8) if emit else None
        cx.inc(g.dma_start(out=wob[:], in_=srco), "sW", "w_o", 16)
        gbar(g, "m_o")
        if KPHASE < 3:
            return
        for p in (0, 1):
            cx.wait(g, f"Tck{2 * (p + 2) + 1}b")
            src = wqt_d[p * 256:(p + 1) * 256, :].rearrange("(c q) d -> q c d", c=2) if emit else None
            cx.inc(g.dma_start(out=wstage[:, p % 2], in_=src), "sW", f"w_qt{p}", 16)
            gbar(g, f"m_qt{p}")
        if KCC:
            cx.wait(g, "Vkv2")
            cx.inc(g.dma_start(out=ccin[:, :], in_=kvcc[:]), "sW", "w_cci", 16)
            cx.waitv(g, "sW", 16 * (cx.cnt["sW"] // 16))
            res = g.collective_compute(
                "AllReduce", ALU.add,
                replica_groups=([[i] for i in range(8)] if KLOCAL else [[0, 1, 2, 3], [4, 5, 6, 7]]),
                ins=[ccin.ap().opt()] if emit else None,
                outs=[ccout.ap().opt()] if emit else None,
            ) if emit else None
            cx.inc(res, "sCC", "CC", 1)
        for p in (2, 3):
            cx.wait(g, f"Tcq{2 * (p - 2) + 1}b")
            src = wqt_d[p * 256:(p + 1) * 256, :].rearrange("(c q) d -> q c d", c=2) if emit else None
            cx.inc(g.dma_start(out=wstage[:, p % 2], in_=src), "sW", f"w_qt{p}", 16)
            gbar(g, f"m_qt{p}")
        if KKRON and KKLVL >= 3:
          for w, dram in (("R", wrdram), ("I", widram)):
            for q in range(8):
                cx.wait(g, f"Vkron_{w}{q}")
                cx.inc(g.dma_start(out=dram[q * 128:(q + 1) * 128, :], in_=kronst[:, q]),
                       "sW", f"w_kd_{w}{q}", 16)
            gbar(g, f"m_kd{w}")
        if KCC:
            cx.wait(g, "CC")
            cx.inc(g.dma_start(out=kvcc[:], in_=ccout[:, :]), "sW", "w_cco", 16)
            gbar(g, "m_cco")
        if KKRON and KKLVL >= 4:
          for w, dram in (("R", wrdram), ("I", widram)):
            src = dram[:, :].rearrange("(b d) (g e) -> (d g) b e", b=32, d=32, g=16, e=16) if emit else None
            for k in range(4):
                if w == "R":
                    cx.inc(g.dma_start(out=wffn[:, k, 0:512], in_=src[k * 128:(k + 1) * 128] if emit else None),
                           "sW", f"w_fixRa{k}", 16)
                    cx.inc(g.dma_start(out=wffn[:, 4 + k, 512:1024], in_=src[k * 128:(k + 1) * 128] if emit else None),
                           "sW", f"w_fixRb{k}", 16)
                else:
                    cx.inc(g.dma_start(out=wffn[:, k, 512:1024], in_=src[k * 128:(k + 1) * 128] if emit else None),
                           "sW", f"w_fixIa{k}", 16)
                    cx.inc(g.dma_start(out=ffntmp[:, k], in_=src[k * 128:(k + 1) * 128] if emit else None),
                           "sW", f"w_fixIb{k}", 16)
        if KKRON and KKLVL >= 4:
            gbar(g, "m_fix")
        if KPHASE < 5:
            return
        for t in range(T):
            if t == 0:
                cx.wait(g, "Vb_bo")
                cx.wait(g, "Vb_bri")
            xwait(g, T + t)
            cx.inc(g.tensor_tensor(out=xbuf[:, t % 2], in0=xbuf[:, t % 2], in1=bo_bc[:], op=ALU.add),
                   "sP", f"Pbo{t}")
            if t >= 1:
                cx.wait(g, f"Vln2_{t - 1}")
                cx.inc(g.tensor_tensor(out=x2[:, (t - 1) % 2], in0=x2[:, (t - 1) % 2], in1=bri_bc[:], op=ALU.add),
                       "sP", f"Pbri{t - 1}")
        cx.wait(g, f"Vln2_{T - 1}")
        cx.inc(g.tensor_tensor(out=x2[:, (T - 1) % 2], in0=x2[:, (T - 1) % 2], in1=bri_bc[:], op=ALU.add),
               "sP", f"Pbri{T - 1}")

    def prog_sy(sy):
        if KPHASE < 2:
            return
        for t in range(T):
            if t >= 2:
                cx.wait(sy, f"Vln{t - 2}")
                cx.wait(sy, f"Sln{t - 2}")
            cx.inc(sy.dma_start(out=xbuf[:, t % 2], in_=x_d[t * NT:(t + 1) * NT, :]),
                   f"sX{t % 2}", f"X{t}", 16)
        if KPHASE < 5:
            return
        for t in range(T):
            if t >= 2:
                cx.wait(sy, f"Vwo{t - 2}")
            else:
                cx.wait(sy, f"Vln{14 + t}")
                cx.wait(sy, f"Sln{14 + t}")
            if t >= 2:
                cx.wait(sy, f"Vout{t - 2}")
                cx.inc(sy.dma_start(out=y_d[(t - 2) * NT:(t - 1) * NT, :], in_=outb[:, t % 2]),
                       f"sO{t % 2}", f"O{t - 2}", 16)
            cx.inc(sy.dma_start(out=xbuf[:, t % 2], in_=x_d[t * NT:(t + 1) * NT, :]),
                   f"sX{t % 2}", f"XB{t}", 16)
        for t in (T - 2, T - 1):
            cx.wait(sy, f"Vout{t}")
            cx.inc(sy.dma_start(out=y_d[t * NT:(t + 1) * NT, :], in_=outb[:, t % 2]),
                   f"sO{t % 2}", f"O{t}", 16)
        if KDBG:
            ykv = t_["ykv"] if emit else None
            yrcp = t_["yrcp"] if emit else None
            yqf = t_["yqf"] if emit else None
            cx.inc(sy.dma_start(out=ykv[:, :] if emit else None, in_=kvcc[:]), "sO0", "Odbg1", 16)
            cx.inc(sy.dma_start(out=yrcp[:, :] if emit else None, in_=recip[:]), "sO0", "Odbg2", 16)
            cx.inc(sy.dma_start(out=yqf[:, :] if emit else None, in_=ring[:, 16]), "sO0", "Odbg3", 16)
            cx.inc(sy.dma_start(out=t_["ykfa"][:, :] if emit else None, in_=kf_acc[:]), "sO0", "Odbg4", 16)
        cx.waitv(sy, "sO0", 16 * (T // 2) + (64 if KDBG else 0))
        cx.waitv(sy, "sO1", 16 * (T // 2))

    def ln_chain(s, par, phase, t):
        s.activation(out=st(par, 2), in_=st(par, 0), func=AF.Identity, scale=1.0 / D, bias=zerov[:] if emit else None)
        s.activation(out=st(par, 3), in_=st(par, 2), func=AF.Square)
        s.activation(out=st(par, 4), in_=st(par, 3), func=AF.Identity, scale=-1.0, bias=epsv[:] if emit else None)
        s.activation(out=st(par, 5), in_=st(par, 1), func=AF.Identity, scale=1.0 / D, bias=st(par, 4))
        s.activation(out=st(par, 6), in_=st(par, 5), func=AF.Ln)
        res = s.activation(out=st(par, 7), in_=st(par, 6), func=AF.Exp, scale=-0.5, bias=zerov[:] if emit else None)
        cx.inc(res, "sS", f"S{phase}{t}")

    def prog_s(s):
        cx.wait(s, "Vconst")
        if KPHASE < 2:
            return

        def s_stats(t, buf, phase):
            xwait(s, t) if phase == "ln" else cx.wait(s, f"Vwo{t}")
            s.activation(out=junk[:], in_=buf[:, t % 2], func=AF.Square, accum_out=st(t, 1))
            s.activation(out=junk[:], in_=buf[:, t % 2], func=AF.Identity, scale=1.0,
                         bias=zerov[:] if emit else None, accum_out=st(t, 0))
            ln_chain(s, t, phase, t)

        s_stats(0, xbuf, "ln")
        for t in range(T):
            if t + 1 < T:
                s_stats(t + 1, xbuf, "ln")
            cx.wait(s, f"Vkf{t}")
            if t >= 2:
                cx.wait(s, f"Tkv{t - 2}")
                cx.wait(s, f"Vacc{t - 2}")
            res = s.activation(out=kfbuf[:, t % 2], in_=tmpbuf[:, t % 2], func=AF.Relu)
            cx.inc(res, "sS", f"Skf{t}")
        if KPHASE < 3:
            return
        for t in range(T):
            cx.wait(s, f"Vqf{t}")
            cx.wait(s, f"Tqtr{t - 2}" if t >= 2 else f"Tkv{T - 1}")
            res = s.activation(out=kfbuf[:, t % 2], in_=tmpbuf[:, t % 2], func=AF.Relu)
            cx.inc(res, "sS", f"Sqf{t}")
        if KPHASE < 5:
            return
        for t in range(T):
            s_stats(t, x2, "ln2")

    def prog_v(v):
        v.memset(zerov[:], 0.0)
        v.memset(epsv[:], 1e-5)
        v.memset(ones1[:], 1.0)
        v.memset(ones128[:], 1.0)
        v.memset(kf_acc[:], 0.0)
        v.memset(kvcc[:, 0:1024], 0.0)
        cx.inc(v.memset(ksumbd[:], 0.0), "sV", "Vconst")
        for nm, dst in (("bo", bo_bc), ("bf", bfr_bc), ("bri", bri_bc)):
            cx.wait(v, f"Tb_{nm}")
            cx.inc(v.tensor_copy(dst[:], psA[:, 0:1024]), "sV", f"Vb_{nm}")
        for c in range(NCH):
            ps = psA if c % 2 == 0 else psB
            for h in range(2):
                cx.wait(v, f"Tck{c}{'ab'[h]}")
                dst = wkf[:, 4 * h:4 * h + 4, c * 128:(c + 1) * 128] if emit else None
                srcv = ps[:, h * 512:(h + 1) * 512].rearrange("p (r q) -> p r q", q=128) if emit else None
                cx.inc(v.tensor_copy(dst, srcv), "sV", f"Vconsk{c}h{h}")
        if KPHASE < 2:
            return
        def v_apply(t):
            xwait(v, t)
            cx.wait(v, f"Sln{t}")
            if t >= 2:
                cx.wait(v, f"Ttr{t - 2}")
            cx.inc(v.tensor_scalar(out=lnbuf[:, t % 2], in0=xbuf[:, t % 2], scalar1=st(t, 2),
                                   scalar2=st(t, 7), op0=ALU.subtract, op1=ALU.mult),
                   "sV", f"Vln{t}")

        v_apply(0)
        for t in range(T):
            par = t % 2
            cx.wait(v, f"Ttr{t}")
            cx.inc(v.tensor_copy(ring[:, t], psA[:, 0:1024]), "sV", f"VlnT{t}")
            if t >= 1:
                cx.wait(v, f"Tkv{t - 1}")
                cx.inc(v.tensor_tensor(out=kvcc[:, 0:1024], in0=kvcc[:, 0:1024], in1=psD[:, 0:1024], op=ALU.add),
                       "sV", f"Vkva{t - 1}")
                cx.wait(v, f"Skf{t - 1}")
                cx.inc(v.tensor_tensor(out=kf_acc[:], in0=kf_acc[:], in1=kfbuf[:, (t - 1) % 2], op=ALU.add),
                       "sV", f"Vacc{t - 1}")
            if t + 1 < T:
                v_apply(t + 1)
            cx.wait(v, f"Tkf{t}")
            if t >= 2:
                cx.wait(v, f"Skf{t - 2}")
            cx.inc(v.tensor_tensor(out=tmpbuf[:, par], in0=psB[:, 0:1024], in1=bfr_bc[:], op=ALU.add),
                   "sV", f"Vkf{t}")
            cx.wait(v, f"Tv{t}")
            if t >= 2:
                cx.wait(v, f"Tkv{t - 2}")
            cx.inc(v.tensor_copy(vtbuf[:, par], psC[:, 0:1024]), "sV", f"Vv{t}")
        if KPHASE < 3:
            return
        cx.wait(v, f"Tkv{T - 1}")
        cx.inc(v.tensor_tensor(out=kvcc[:, 0:1024], in0=kvcc[:, 0:1024], in1=psD[:, 0:1024], op=ALU.add),
               "sV", f"Vkva{T - 1}")
        cx.wait(v, f"Skf{T - 1}")
        cx.inc(v.tensor_tensor(out=kf_acc[:], in0=kf_acc[:], in1=kfbuf[:, (T - 1) % 2], op=ALU.add),
               "sV", f"Vacc{T - 1}")
        cx.wait(v, "Tks")
        cx.inc(v.tensor_copy(kvcc[:, 1024:1032], psC[:, 0:8]), "sV", "Vkv2")
        for c in range(NCH):
            ps = psA if c % 2 == 0 else psB
            for h in range(2):
                cx.wait(v, f"Tcq{c}{'ab'[h]}")
                dst = wqf[:, 4 * h:4 * h + 4, c * 128:(c + 1) * 128] if emit else None
                srcv = ps[:, h * 512:(h + 1) * 512].rearrange("p (r q) -> p r q", q=128) if emit else None
                cx.inc(v.tensor_copy(dst, srcv), "sV", f"Vconsq{c}h{h}")
        if KKRON and KKLVL >= 2:
          for w in ("R", "I"):
            for q in range(8):
                cx.wait(v, f"Tkron_{w}{q}")
                if w == "I" and q == 0 and KKLVL >= 3:
                    cx.wait(v, "m_kdR")
                src = psC[:, 0:256] if q % 2 == 0 else psC[:, 512:768]
                cx.inc(v.tensor_copy(kronst[:, q], src), "sV", f"Vkron_{w}{q}")
        def v_a3prep():
            for k in range(4):
                cx.wait(v, "m_fix")
                cx.inc(v.tensor_scalar(out=wffn[:, 4 + k, 0:512], in0=ffntmp[:, k], scalar1=-1.0,
                                       scalar2=0.0, op0=ALU.mult, op1=ALU.add),
                       "sV", f"Vneg{k}")
            cx.wait(v, "m_cco")
            in0 = kvcc[:, 0:1024].rearrange("p (j q) -> p j q", j=8) if emit else None
            in1 = maskbd_s[:].unsqueeze(1).broadcast_to([128, 8, 128]) if emit else None
            out0 = kvbd[:].rearrange("p (j q) -> p j q", j=8) if emit else None
            v.tensor_tensor(out=out0, in0=in0, in1=in1, op=ALU.mult)
            for c in range(NCH):
                v.tensor_scalar(out=ksumbd[:, c, 2 * c:2 * c + 1], in0=kvcc[:, 1024 + c:1025 + c],
                                scalar1=mlow[:] if emit else None, scalar2=None, op0=ALU.mult)
                v.tensor_scalar(out=ksumbd[:, c, 2 * c + 1:2 * c + 2], in0=kvcc[:, 1024 + c:1025 + c],
                                scalar1=mhigh[:] if emit else None, scalar2=None, op0=ALU.mult)
            v.memset(vjunk[:], 0.0)
            cx.inc(v.memset(vjunk[:], 0.0), "sV", "VA3a")

        for t in range(T):
            par = t % 2
            cx.wait(v, f"Tqf{t}")
            cx.wait(v, f"Sqf{t - 2}" if t >= 2 else f"Skf{T - 1}")
            psq = psB if t % 2 == 0 else psD
            cx.inc(v.tensor_tensor(out=tmpbuf[:, par], in0=psq[:, 0:1024], in1=bfr_bc[:], op=ALU.add),
                   "sV", f"Vqf{t}")
            cx.wait(v, f"Tqtr{t}")
            slot = (t + T) % (T + 1)
            cx.inc(v.tensor_copy(ring[:, slot], psA[:, 0:1024]), "sV", f"VqT{t}")
            if t == 8 and KPHASE >= 4:
                v_a3prep()
        if KPHASE < 4:
            return
        cx.wait(v, "Tnorm")
        for t in range(T):
            cx.inc(v.tensor_scalar(out=normc[:, t * 16:(t + 1) * 16], in0=psC[:, t * 16:(t + 1) * 16],
                                   scalar1=1e-6, scalar2=None, op0=ALU.max),
                   "sV", f"Vnm{t}")
        v.memset(vjunk[:], 0.0)
        v.memset(vjunk[:], 0.0)
        for t in range(T):
            v.reciprocal(recip[:, t * 16:(t + 1) * 16], normc[:, t * 16:(t + 1) * 16])
        v.memset(vjunk[:], 0.0)
        cx.inc(v.memset(vjunk[:], 0.0), "sV", "Va3end")
        if KPHASE < 5:
            return
        def v_ein(t):
            par = t % 2
            cx.wait(v, f"Tein{t}")
            if t >= 2:
                cx.wait(v, f"Tattr{t - 2}")
            in0 = psA[:, 0:1024].rearrange("p (h d) -> p h d", h=16) if emit else None
            in1 = recip[:, t * 16:(t + 1) * 16].unsqueeze(2).broadcast_to([128, 16, 64]) if emit else None
            out0 = kfbuf[:, par].rearrange("p (h d) -> p h d", h=16) if emit else None
            cx.inc(v.tensor_tensor(out=out0, in0=in0, in1=in1, op=ALU.mult), "sV", f"Vein{t}")

        def v_attnT(t):
            par = t % 2
            cx.wait(v, f"Tattr{t}")
            if t >= 2:
                cx.wait(v, f"Two{t - 2}")
            cx.inc(v.tensor_copy(vtbuf[:, par], psB[:, 0:1024]), "sV", f"VaT{t}")

        def v_apply2(t):
            par = t % 2
            cx.wait(v, f"Sln2{t}")
            if t >= 2:
                cx.wait(v, f"Tx2tr{t - 2}")
            cx.inc(v.tensor_scalar(out=lnbuf[:, par], in0=x2[:, par], scalar1=st(t, 2),
                                   scalar2=st(t, 7), op0=ALU.subtract, op1=ALU.mult),
                   "sV", f"Vln2_{t}")

        def v_stt(t):
            par = t % 2
            cx.wait(v, f"Two{t}")
            cx.wait(v, f"Pbo{t}")
            if t >= 2:
                cx.wait(v, f"Pbri{t - 2}")
                cx.wait(v, f"Sln2{t - 2}")
            cx.inc(v.scalar_tensor_tensor(out=x2[:, par], in0=psC[:, 0:1024], scalar=1.0,
                                          in1=xbuf[:, par], op0=ALU.mult, op1=ALU.add),
                   "sV", f"Vwo{t}")

        def v_x2T(t):
            par = t % 2
            cx.wait(v, f"Tx2tr{t}")
            if t >= 2:
                cx.wait(v, f"Tffn{t - 2}")
            cx.inc(v.tensor_copy(tmpbuf[:, par], psB[:, 0:1024]), "sV", f"Vx2T{t}")

        def v_out(t):
            par = t % 2
            cx.wait(v, f"Tffn{t}")
            cx.wait(v, f"Pbri{t}")
            if t >= 2:
                cx.wait(v, f"O{t - 2}")
            cx.inc(v.scalar_tensor_tensor(out=outb[:, par], in0=psD[:, 0:1024], scalar=1.0,
                                          in1=x2[:, par], op0=ALU.mult, op1=ALU.add),
                   "sV", f"Vout{t}")

        for t in range(T):
            v_ein(t)
            v_attnT(t)
            if t >= 1:
                v_apply2(t - 1)
            v_stt(t)
            if t >= 1:
                v_x2T(t - 1)
                v_out(t - 1)
        v_apply2(T - 1)
        v_x2T(T - 1)
        v_out(T - 1)

    def prog_t(te):
        for nm, row in (("bo", borow_s), ("bf", bfrow_s), ("bri", brirow_s)):
            cx.wait(te, "Vconst")
            cx.wait(te, "m_smalls")
            if nm == "bf":
                cx.wait(te, "Vb_bo")
            if nm == "bri":
                cx.wait(te, "Vb_bf")
            te.matmul(psA[:, 0:512], ones1[:], row[:, 0:512], start=True, stop=True)
            res = te.matmul(psA[:, 512:1024], ones1[:], row[:, 512:1024], start=True, stop=True)
            cx.inc(res, "sT", f"Tb_{nm}")
        cx.wait(te, "m_smalls")
        for c in range(NCH):
            p, cc = c // 2, c % 2
            cx.wait(te, f"m_kt{p}")
            ps = psA if c % 2 == 0 else psB
            if c >= 2:
                cx.wait(te, f"Vconsk{c - 2}h0")
                cx.wait(te, f"Vconsk{c - 2}h1")
            elif c == 0:
                cx.wait(te, "Vb_bri")
            for h in range(2):
                for rq in range(4 * h, 4 * h + 4):
                    lhs = wstage[:, p % 2, cc, rq * 128:(rq + 1) * 128] if emit else None
                    res = te.matmul(ps[:, h * 512 + (rq % 4) * 128: h * 512 + (rq % 4) * 128 + 128],
                                    lhs, bd2wf[:], start=(rq % 4 == 0), stop=(rq % 4 == 3))
                    if rq % 4 == 3:
                        cx.inc(res, "sT", f"Tck{c}{'ab'[h]}")
        if KPHASE < 2:
            return
        def emit_kv(t):
            cx.wait(te, f"Skf{t}")
            cx.wait(te, f"Vv{t}")
            if t >= 1:
                cx.wait(te, f"Vkva{t - 1}")
            for j in range(8):
                res = te.matmul(psD[:, j * 128:(j + 1) * 128],
                                kfbuf[:, t % 2, j * 128:(j + 1) * 128] if emit else None,
                                vtbuf[:, t % 2, j * 128:(j + 1) * 128] if emit else None,
                                start=(j % 4 == 0), stop=(j % 4 == 3))
            cx.inc(res, "sT", f"Tkv{t}")

        def emit_tr(t):
            cx.wait(te, f"Vln{t}")
            cx.wait(te, f"VlnT{t - 1}" if t >= 1 else "Vconsk6h0")
            if t == 0:
                cx.wait(te, "Vconsk6h1")
                cx.wait(te, "w_ident")
            for c in range(NCH):
                res = te.matmul(psA[:, c * 128:(c + 1) * 128], lnbuf[:, t % 2, c * 128:(c + 1) * 128] if emit else None,
                                identb[:], start=(c % 4 == 0), stop=(c % 4 == 3))
            cx.inc(res, "sT", f"Ttr{t}")

        emit_tr(0)
        for t in range(T):
            par = t % 2
            if t >= 1:
                emit_kv(t - 1)
            cx.wait(te, f"VlnT{t}")
            if t >= 1:
                cx.wait(te, f"Vkf{t - 1}")
            else:
                cx.wait(te, "Vconsk7h0")
                cx.wait(te, "Vconsk7h1")
            for h in range(2):
                for k in range(NCH):
                    res = te.matmul(psB[:, h * 512:(h + 1) * 512], ring[:, t, k * 128:(k + 1) * 128] if emit else None,
                                    wkf[:, k, h * 512:(h + 1) * 512] if emit else None,
                                    start=(k == 0), stop=(k == NCH - 1))
            cx.inc(res, "sT", f"Tkf{t}")
            if t == 0:
                cx.wait(te, "m_v")
            if t >= 1:
                cx.wait(te, f"Vv{t - 1}")
            for h in range(2):
                for k in range(NCH):
                    res = te.matmul(psC[:, h * 512:(h + 1) * 512], ring[:, t, k * 128:(k + 1) * 128] if emit else None,
                                    wvb[:, k, h * 512:(h + 1) * 512] if emit else None,
                                    start=(k == 0), stop=(k == NCH - 1))
            cx.inc(res, "sT", f"Tv{t}")
            if t + 1 < T:
                emit_tr(t + 1)
        emit_kv(T - 1)
        if KPHASE < 3:
            return
        for c in range(NCH):
            if c == 2:
                cx.wait(te, f"Vacc{T - 1}")
                for cc2 in range(NCH):
                    res = te.matmul(psC[:, cc2:cc2 + 1], kf_acc[:, cc2 * 128:(cc2 + 1) * 128] if emit else None,
                                    ones128[:], start=(cc2 == 0), stop=(cc2 == NCH - 1))
                cx.inc(res, "sT", "Tks")
            p, cc = c // 2, c % 2
            cx.wait(te, f"m_qt{p}")
            ps = psA if c % 2 == 0 else psB
            if c >= 2:
                cx.wait(te, f"Vconsq{c - 2}h0")
                cx.wait(te, f"Vconsq{c - 2}h1")
            elif c == 0:
                cx.wait(te, f"VlnT{T - 1}")
            else:
                cx.wait(te, f"Vkf{T - 1}")
            for h in range(2):
                for rq in range(4 * h, 4 * h + 4):
                    lhs = wstage[:, p % 2, cc, rq * 128:(rq + 1) * 128] if emit else None
                    res = te.matmul(ps[:, h * 512 + (rq % 4) * 128: h * 512 + (rq % 4) * 128 + 128],
                                    lhs, bd2wf[:], start=(rq % 4 == 0), stop=(rq % 4 == 3))
                    if rq % 4 == 3:
                        cx.inc(res, "sT", f"Tcq{c}{'ab'[h]}")
        if KKRON:
          for w, (a, b) in (("R", (aRb, bRb)), ("I", (aIb, bIb))):
            cx.wait(te, "m_smalls")
            for q in range(8):
                if q >= 2 and KKLVL >= 2:
                    cx.wait(te, f"Vkron_{w}{q - 2}")
                elif w == "R":
                    cx.wait(te, "Vkv2")
                elif KKLVL >= 2:
                    cx.wait(te, f"Vkron_R{6 + q}")
                dst = psC[:, 0:256] if q % 2 == 0 else psC[:, 512:768]
                res = te.matmul(dst, a[:, q * 128:(q + 1) * 128], b[:], start=True, stop=True)
                cx.inc(res, "sT", f"Tkron_{w}{q}")
        def emit_norm_tile(t2, last=False):
            cx.wait(te, f"VqT{t2}")
            slot2 = (t2 + T) % (T + 1)
            for c2 in range(NCH):
                res = te.matmul(psC[:, t2 * 16:(t2 + 1) * 16], ring[:, slot2, c2 * 128:(c2 + 1) * 128] if emit else None,
                                ksumbd[:, c2], start=(c2 == 0), stop=(c2 == NCH - 1))
            if last:
                cx.inc(res, "sT", "Tnorm")

        def emit_qf(t):
            ps = psB if t % 2 == 0 else psD
            if t >= 2:
                cx.wait(te, f"Vqf{t - 2}")
            elif t == 0:
                cx.wait(te, "Vconsq7h1")
                cx.wait(te, "Vconsq7h0")
            else:
                cx.wait(te, f"Vkva{T - 1}")
            for h in range(2):
                for k in range(NCH):
                    res = te.matmul(ps[:, h * 512:(h + 1) * 512], ring[:, t, k * 128:(k + 1) * 128] if emit else None,
                                    wqf[:, k, h * 512:(h + 1) * 512] if emit else None,
                                    start=(k == 0), stop=(k == NCH - 1))
            cx.inc(res, "sT", f"Tqf{t}")

        emit_qf(0)
        for t in range(T):
            if t == 10 and KPHASE >= 4:
                cx.wait(te, "VA3a")
                cx.wait(te, "Vkv2")
                if KKRON:
                    cx.wait(te, "Vkron_I7")
                for t2 in range(9):
                    emit_norm_tile(t2)
            if t >= 11 and KPHASE >= 4:
                emit_norm_tile(t - 2)
            if t + 1 < T:
                emit_qf(t + 1)
            cx.wait(te, f"Sqf{t}")
            cx.wait(te, f"VqT{t - 1}" if t >= 1 else "Vconsq6h0")
            if t == 0:
                cx.wait(te, "Vconsq6h1")
            for c in range(NCH):
                res = te.matmul(psA[:, c * 128:(c + 1) * 128],
                                kfbuf[:, t % 2, c * 128:(c + 1) * 128] if emit else None,
                                identb[:], start=(c % 4 == 0), stop=(c % 4 == 3))
            cx.inc(res, "sT", f"Tqtr{t}")
        if KPHASE >= 4:
            emit_norm_tile(T - 2)
            emit_norm_tile(T - 1, last=True)
        if KPHASE < 5:
            return
        def emit_ein(t):
            cx.wait(te, "Va3end" if t == 0 else f"Vein{t - 1}")
            if t == 0:
                cx.wait(te, f"VqT{T - 1}")
            slot = (t + T) % (T + 1)
            for c in range(NCH):
                res = te.matmul(psA[:, c * 128:(c + 1) * 128], ring[:, slot, c * 128:(c + 1) * 128] if emit else None,
                                kvbd[:, c * 128:(c + 1) * 128] if emit else None,
                                start=(c % 4 == 0), stop=(c % 4 == 3))
            cx.inc(res, "sT", f"Tein{t}")

        def emit_attnT(t):
            cx.wait(te, f"Vein{t}")
            cx.wait(te, f"Vx2T{t - 2}" if t >= 2 else f"Vqf{T - 1}")
            for c in range(NCH):
                res = te.matmul(psB[:, c * 128:(c + 1) * 128],
                                kfbuf[:, t % 2, c * 128:(c + 1) * 128] if emit else None,
                                identb[:], start=(c % 4 == 0), stop=(c % 4 == 3))
            cx.inc(res, "sT", f"Tattr{t}")

        def emit_wo(t):
            cx.wait(te, f"VaT{t}")
            cx.wait(te, f"Vwo{t - 1}" if t >= 1 else "Va3end")
            if t == 0:
                cx.wait(te, "m_o")
            for h in range(2):
                for k in range(NCH):
                    res = te.matmul(psC[:, h * 512:(h + 1) * 512], vtbuf[:, t % 2, k * 128:(k + 1) * 128] if emit else None,
                                    wob[:, k, h * 512:(h + 1) * 512] if emit else None,
                                    start=(k == 0), stop=(k == NCH - 1))
            cx.inc(res, "sT", f"Two{t}")

        def emit_x2tr(t):
            cx.wait(te, f"Vln2_{t}")
            cx.wait(te, f"VaT{t + 1}" if t + 1 < T else f"VaT{t}")
            for c in range(NCH):
                res = te.matmul(psB[:, c * 128:(c + 1) * 128],
                                lnbuf[:, t % 2, c * 128:(c + 1) * 128] if emit else None,
                                identb[:], start=(c % 4 == 0), stop=(c % 4 == 3))
            cx.inc(res, "sT", f"Tx2tr{t}")

        def emit_ffn(t):
            cx.wait(te, f"Vx2T{t}")
            cx.wait(te, f"Vout{t - 1}" if t >= 1 else f"Vqf{T - 1}")
            if t == 0:
                cx.wait(te, "m_fix")
                for k in range(4):
                    cx.wait(te, f"Vneg{k}")
            for h in range(2):
                for k in range(NCH):
                    res = te.matmul(psD[:, h * 512:(h + 1) * 512], tmpbuf[:, t % 2, k * 128:(k + 1) * 128] if emit else None,
                                    wffn[:, k, h * 512:(h + 1) * 512] if emit else None,
                                    start=(k == 0), stop=(k == NCH - 1))
            cx.inc(res, "sT", f"Tffn{t}")

        for t in range(T):
            emit_ein(t)
            emit_attnT(t)
            emit_wo(t)
            if t >= 1:
                emit_x2tr(t - 1)
                emit_ffn(t - 1)
        emit_x2tr(T - 1)
        emit_ffn(T - 1)

    if emit:
        block = cx.block
        block.gpsimd(prog_g)
        block.sync(prog_sy)
        block.scalar(prog_s)
        block.vector(prog_v)
        block.tensor(prog_t)
    else:
        prog_g(gE)
        prog_sy(syE)
        prog_s(sE)
        prog_v(vE)
        prog_t(tE)


def make_nc(EV):
    nc = bass.Bass("TRN2", target_bir_lowering=False, debug=False)
    nc.detect_race_conditions = False
    t_ = {}
    t_["x_d"] = nc.dram_tensor("x", [TOK, D], F32, kind="ExternalInput").ap()
    t_["y_d"] = nc.dram_tensor("y", [TOK, D], F32, kind="ExternalOutput").ap()
    if KDBG:
        t_["ykv"] = nc.dram_tensor("ykv", [128, 1032], F32, kind="ExternalOutput").ap()
        t_["yrcp"] = nc.dram_tensor("yrcp", [128, 256], F32, kind="ExternalOutput").ap()
        t_["yqf"] = nc.dram_tensor("yqf", [128, D], mybir.dt.bfloat16, kind="ExternalOutput").ap()
        t_["ykfa"] = nc.dram_tensor("ykfa", [128, D], F32, kind="ExternalOutput").ap()
    t_["wqt_d"] = nc.dram_tensor("wqt", [D, D], F32, kind="ExternalInput").ap()
    t_["wkt_d"] = nc.dram_tensor("wkt", [D, D], F32, kind="ExternalInput").ap()
    t_["wv_d"] = nc.dram_tensor("wv", [D, D], F32, kind="ExternalInput").ap()
    t_["wo_d"] = nc.dram_tensor("wo", [D, D], F32, kind="ExternalInput").ap()
    t_["wf_d"] = nc.dram_tensor("wf", [64, 64], F32, kind="ExternalInput").ap()
    t_["bfrow_d"] = nc.dram_tensor("bfrow", [1, D], F32, kind="ExternalInput").ap()
    t_["borow_d"] = nc.dram_tensor("borow", [1, D], F32, kind="ExternalInput").ap()
    t_["brirow_d"] = nc.dram_tensor("brirow", [1, D], F32, kind="ExternalInput").ap()
    t_["aR_d"] = nc.dram_tensor("aR", [4, 1024], F32, kind="ExternalInput").ap()
    t_["bR_d"] = nc.dram_tensor("bR", [4, 256], F32, kind="ExternalInput").ap()
    t_["aI_d"] = nc.dram_tensor("aI", [4, 1024], F32, kind="ExternalInput").ap()
    t_["bI_d"] = nc.dram_tensor("bI", [4, 256], F32, kind="ExternalInput").ap()
    t_["ident_d"] = nc.dram_tensor("ident", [128, 128], F32, kind="ExternalInput").ap()
    t_["maskbd_d"] = nc.dram_tensor("maskbd", [128, 128], F32, kind="ExternalInput").ap()
    t_["mlow_d"] = nc.dram_tensor("mlow_in", [128, 1], F32, kind="ExternalInput").ap()
    t_["mhigh_d"] = nc.dram_tensor("mhigh_in", [128, 1], F32, kind="ExternalInput").ap()
    t_["ccin"] = nc.dram_tensor("ccin", [128, 1032], F32)
    t_["ccout"] = nc.dram_tensor("ccout", [128, 1032], F32)
    t_["wrdram"] = nc.dram_tensor("wrdram", [1024, 256], F32)
    t_["widram"] = nc.dram_tensor("widram", [1024, 256], F32)

    from contextlib import ExitStack
    ctx = ExitStack()
    E = ctx.enter_context

    def sb(name, shape, dt):
        t_[name] = E(nc.sbuf_tensor(name, shape, dt))

    sb("wkf", [128, NCH, D], BF16)
    t_["wqf"] = t_["wkf"]
    sb("wvb", [128, NCH, D], BF16)
    sb("wob", [128, NCH, D], BF16)
    sb("wffn", [128, NCH, D], BF16)
    sb("wstage", [128, 2, 2, D], BF16)
    sb("ring", [128, T + 1, D], BF16)
    sb("xbuf", [128, 2, D], F32)
    sb("lnbuf", [128, 2, D], BF16)
    sb("kfbuf", [128, 2, D], BF16)
    sb("vtbuf", [128, 2, D], BF16)
    sb("tmpbuf", [128, 2, D], BF16)
    sb("x2", [128, 2, D], F32)
    sb("outb", [128, 2, D], F32)
    sb("bo_bc", [128, D], F32)
    sb("bfr_bc", [128, D], F32)
    sb("bri_bc", [128, D], F32)
    sb("kvcc", [128, 1032], F32)
    sb("kf_acc", [128, D], F32)
    sb("kvbd", [128, D], BF16)
    sb("ksumbd", [128, 8, 16], BF16)
    sb("normc", [128, 256], F32)
    sb("recip", [128, 256], F32)
    sb("junk", [128, D], BF16)
    sb("vjunk", [128, 16], F32)
    sb("kronst", [128, 8, 256], F32)
    sb("ffntmp", [128, 4, 512], BF16)
    sb("identb", [128, 128], BF16)
    sb("bd2wf", [128, 128], BF16)
    sb("maskbd_s", [128, 128], F32)
    sb("mlow", [128, 1], F32)
    sb("mhigh", [128, 1], F32)
    sb("aRb", [4, 1024], BF16)
    sb("bRb", [4, 256], BF16)
    sb("aIb", [4, 1024], BF16)
    sb("bIb", [4, 256], BF16)
    sb("borow_s", [1, D], F32)
    sb("bfrow_s", [1, D], F32)
    sb("brirow_s", [1, D], F32)
    sb("ones1", [1, 128], F32)
    sb("ones128", [128, 1], F32)
    sb("zerov", [128, 1], F32)
    sb("epsv", [128, 1], F32)
    sb("stA", [128, 16], F32)
    sb("stB", [128, 16], F32)

    def ps(name):
        t_[name] = E(nc.psum_tensor(name, [128, 512 * 2], F32))

    ps("psA"); ps("psB"); ps("psC"); ps("psD")

    sems = {}
    for s in ("sX0", "sX1", "sO0", "sO1", "sW", "sCC", "sT", "sV", "sS", "sP", "sM"):
        sems[s] = E(nc.semaphore(s))
    block = E(nc.Block())

    cx = Ctx(True, EV, nc=nc, sems=sems, tensors=t_)
    cx.block = block
    cx.block_engines = (None,) * 5
    build_program(cx)
    ctx.close()
    return nc


def kernel(L0_attn_x, L0_attn_Wq, L0_attn_Wk, L0_attn_Wv, L0_attn_Wo, L0_attn_bo,
           L0_attn_Wf, L0_attn_bf, L0_ffn_G1_R, L0_ffn_G2_R, L0_ffn_G1_I, L0_ffn_G2_I,
           L0_ffn_bias_real, L0_ffn_bias_imag):
    f = np.float32
    x2d = np.ascontiguousarray(np.asarray(L0_attn_x, dtype=f).reshape(TOK * NCORES, D))
    wqt = np.ascontiguousarray(np.asarray(L0_attn_Wq, dtype=f).T)
    wkt = np.ascontiguousarray(np.asarray(L0_attn_Wk, dtype=f).T)
    wv = np.ascontiguousarray(np.asarray(L0_attn_Wv, dtype=f))
    wo = np.ascontiguousarray(np.asarray(L0_attn_Wo, dtype=f))
    wf = np.ascontiguousarray(np.asarray(L0_attn_Wf, dtype=f))
    bfrow = np.ascontiguousarray(np.tile(np.asarray(L0_attn_bf, dtype=f), 16).reshape(1, D))
    borow = np.ascontiguousarray(np.asarray(L0_attn_bo, dtype=f).reshape(1, D))
    brirow = np.ascontiguousarray(
        np.concatenate([np.asarray(L0_ffn_bias_real, dtype=f), np.asarray(L0_ffn_bias_imag, dtype=f)]).reshape(1, D))
    aR = np.ascontiguousarray(np.asarray(L0_ffn_G1_R, dtype=f)[0].transpose(1, 0, 2).reshape(4, 1024))
    bR = np.ascontiguousarray(np.asarray(L0_ffn_G2_R, dtype=f)[:, :, 0, :].transpose(0, 2, 1).reshape(4, 256))
    aI = np.ascontiguousarray(np.asarray(L0_ffn_G1_I, dtype=f)[0].transpose(1, 0, 2).reshape(4, 1024))
    bI = np.ascontiguousarray(np.asarray(L0_ffn_G2_I, dtype=f)[:, :, 0, :].transpose(0, 2, 1).reshape(4, 256))
    ident = np.eye(128, dtype=f)
    maskbd = np.kron(np.eye(2, dtype=f), np.ones((64, 64), dtype=f))
    mlow = np.zeros((128, 1), dtype=f); mlow[:64] = 1.0
    mhigh = np.zeros((128, 1), dtype=f); mhigh[64:] = 1.0

    EV = {}
    build_program(Ctx(False, EV))
    nc = make_nc(EV)

    shared = dict(wqt=wqt, wkt=wkt, wv=wv, wo=wo, wf=wf, bfrow=bfrow, borow=borow,
                  brirow=brirow, aR=aR, bR=bR, aI=aI, bI=bI, ident=ident,
                  maskbd=maskbd, mlow_in=mlow, mhigh_in=mhigh)
    in_maps = []
    for i in range(NCORES):
        m = dict(shared)
        m["x"] = np.ascontiguousarray(x2d[i * TOK:(i + 1) * TOK])
        in_maps.append(m)
    res = run_bass_kernel_spmd(nc, in_maps, core_ids=list(range(NCORES)))
    out = np.concatenate([np.asarray(res.results[i]["y"]) for i in range(NCORES)], axis=0)
    return out.reshape(2, 8192, D)
